# version 02
# baseline (speedup 1.0000x reference)
import numpy as np
import ml_dtypes
import concourse.bass as bass
import concourse.mybir as mybir
from concourse.bass_utils import run_bass_kernel_spmd

F32 = mybir.dt.float32
BF16 = mybir.dt.bfloat16
AF = mybir.ActivationFunctionType
ALU = mybir.AluOpType
AX = mybir.AxisListType.X

DM = 4096
NCH = 32
LSEQ = 2048
LOWN = 1024
NH = 16
HD = 256
INT = 8192
LAT = 2048
NE = 8
EPS = 1e-5
SCALE = 1.0 / 16.0

ENGINES = ("sync", "tensor", "vector", "scalar", "gpsimd")


def _build_tables():
    g = [1.0, 1.0, 1.0]
    dim, n = 3, 8
    masks = sorted(range(n), key=lambda m: (bin(m).count("1"), m))
    inv = {m: i for i, m in enumerate(masks)}
    C = np.zeros((n, n, n), dtype=np.float64)
    for i, a in enumerate(masks):
        for j, b in enumerate(masks):
            sw, at = 0, a >> 1
            while at:
                sw += bin(at & b).count("1")
                at >>= 1
            s = -1.0 if sw & 1 else 1.0
            for t in range(dim):
                if a >> t & 1 and b >> t & 1:
                    s *= g[t]
            C[i, j, inv[a ^ b]] = s
    grades = np.array([bin(m).count("1") for m in masks])
    rev = np.where(grades * (grades - 1) // 2 % 2 == 1, -1.0, 1.0)
    met = np.array(
        [np.prod([g[t] for t in range(dim) if m >> t & 1] + [1.0]) for m in masks]
    )
    return C, rev, met


def _host_consts():
    C, rev, met = _build_tables()
    biv = C[:, :, [4, 5, 6]]
    c1 = np.zeros((3, 64), dtype=np.float32)
    for c in range(3):
        for i in range(8):
            for j in range(8):
                c1[c, i * 8 + j] = rev[i] * biv[i, j, c]
    dconst = np.zeros((64, 16), dtype=np.float32)
    for i in range(8):
        dconst[i * 8 + i, :] = SCALE * rev[i] * met[i]
    repm = np.zeros((8, 128), dtype=np.float32)
    for d in range(16):
        for i in range(8):
            repm[i, d * 8 + i] = 1.0
    maskm = np.zeros((128, 128), dtype=np.float32)
    for d in range(16):
        maskm[d * 8:(d + 1) * 8, d * 8:(d + 1) * 8] = 1.0
    pairs = np.zeros((8, 28), dtype=np.float32)
    k = 0
    for i in range(8):
        for j in range(i + 1, 8):
            pairs[i, k] = 1.0
            pairs[j, k] = 1.0
            k += 1
    return c1, dconst, repm, maskm, pairs


class _Sem:
    __slots__ = ("h", "v", "name")

    def __init__(self, h, name):
        self.h = h
        self.v = 0
        self.name = name


class Prog:
    def __init__(self, nc, stack):
        self.nc = nc
        self.stack = stack
        self.items = {e: [] for e in ENGINES}
        self.sems = []
        self.last_real = {e: None for e in ENGINES}
        self._esems = {}

    def sem(self, name):
        h = self.stack.enter_context(self.nc.semaphore(name))
        s = _Sem(h, name)
        self.sems.append(s)
        return s

    def op(self, eng, fn, waits=(), incs=()):
        item = (list(waits), fn, list(incs))
        self.items[eng].append(item)
        for s, n in incs:
            s.v += n
        if fn is not None:
            self.last_real[eng] = item
        return item

    def eop(self, eng, name, *args, waits=(), incs=(), **kwargs):
        def mk(name=name, args=args, kwargs=kwargs):
            def f(e):
                return getattr(e, name)(*args, **kwargs)
            return f
        return self.op(eng, mk(), waits=waits, incs=incs)

    def barrier(self):
        for e in ENGINES:
            it = self.last_real[e]
            if it is not None and not it[2]:
                s = self._esem(e)
                it[2].append((s, 1))
                s.v += 1
        snap = [(s, s.v) for s in self.sems if s.v > 0]
        for e in ENGINES:
            self.op(e, None, waits=snap)
        self.last_real = {e: None for e in ENGINES}

    def _esem(self, e):
        if e not in self._esems:
            self._esems[e] = self.sem(f"eb_{e}")
        return self._esems[e]

    def playback(self, block):
        def make(e):
            sub = self.items[e]

            def run(eng):
                for waits, fn, incs in sub:
                    for s, v in waits:
                        eng.wait_ge(s.h, v)
                    if fn is None:
                        continue
                    inst = fn(eng)
                    for s, n in incs:
                        inst.then_inc(s.h, n)
            return run

        block.sync(make("sync"))
        block.tensor(make("tensor"))
        block.vector(make("vector"))
        block.scalar(make("scalar"))
        block.gpsimd(make("gpsimd"))


def build_nc(dbg=()):
    import contextlib

    nc = bass.Bass("TRN2", target_bir_lowering=False)
    stack = contextlib.ExitStack()

    def dram(name, shape, dtype, kind):
        if name in dbg:
            kind = "ExternalOutput"
        return nc.dram_tensor(name, shape, dtype, kind=kind).ap()

    xfm = dram("xfm", [DM, LSEQ], F32, "ExternalInput")
    wqT = dram("wqT", [DM, DM], F32, "ExternalInput")
    wkT = dram("wkT", [DM, DM], F32, "ExternalInput")
    wvT = dram("wvT", [DM, DM], F32, "ExternalInput")
    bqc = dram("bqc", [128, NCH], F32, "ExternalInput")
    bkc = dram("bkc", [128, NCH], F32, "ExternalInput")
    bvc = dram("bvc", [128, NCH], F32, "ExternalInput")
    gamc = dram("gamc", [128, NCH], F32, "ExternalInput")
    betc = dram("betc", [128, NCH], F32, "ExternalInput")
    bwT = dram("bwT", [3, 16], F32, "ExternalInput")
    gw = dram("gw", [DM, NE], F32, "ExternalInput")
    upw = dram("upw", [DM, INT], F32, "ExternalInput")
    dnw = dram("dnw", [INT, DM], F32, "ExternalInput")
    f1w = dram("f1w", [DM, LAT], F32, "ExternalInput")
    f2w = dram("f2w", [LAT, DM], F32, "ExternalInput")
    ew = dram("ew", [LAT, LAT], F32, "ExternalInput")
    c1 = dram("c1", [3, 64], F32, "ExternalInput")
    dconst = dram("dconst", [64, 16], F32, "ExternalInput")
    repm = dram("repm", [8, 128], F32, "ExternalInput")
    maskm = dram("maskm", [128, 128], F32, "ExternalInput")
    ppm = dram("ppm", [8, 28], F32, "ExternalInput")
    idn_in = dram("idn_in", [128, 128], F32, "ExternalInput")
    ksp = dram("ksp", [DM, LSEQ], BF16, "Internal")
    vsp = dram("vsp", [LSEQ, DM], BF16, "Internal")
    qsp = dram("qsp", [DM, LOWN], BF16, "Internal")
    x2sp = dram("x2sp", [DM, LOWN], F32, "Internal")
    partsp = dram("partsp", [DM, LOWN], F32, "Internal")
    presp = dram("presp", [DM, LOWN], BF16, "Internal")
    adrm = dram("adrm", [16, 64], F32, "Internal")
    out = dram("out", [DM, LOWN], F32, "ExternalOutput")
    dbg4 = dram("dbg4", [128, 4096], F32, "ExternalOutput") if "dbg4" in dbg \
        else None
    dbg5 = dram("dbg5", [128, 4096], F32, "ExternalOutput") if "dbg4" in dbg \
        else None

    BIGT = stack.enter_context(nc.sbuf_tensor("BIG", [128, 65536], BF16))
    WART = stack.enter_context(nc.sbuf_tensor("WAR", [128, 16384], BF16))
    XSTT = stack.enter_context(nc.sbuf_tensor("XST", [128, 4096], F32))
    TBLK = stack.enter_context(nc.sbuf_tensor("TBLK", [128, NH * 128], BF16))[:]
    onesc = stack.enter_context(nc.sbuf_tensor("onesc", [128, 1], BF16))[:]
    onesr = stack.enter_context(nc.sbuf_tensor("onesr", [1, 128], BF16))[:]
    idn = stack.enter_context(nc.sbuf_tensor("idn", [128, 128], F32))[:]
    maskb = stack.enter_context(nc.sbuf_tensor("maskb", [128, 128], BF16))[:]
    ppb = stack.enter_context(nc.sbuf_tensor("ppb", [8, 28], BF16))[:]
    repf = stack.enter_context(nc.sbuf_tensor("repf", [8, 128], F32))[:]
    c1f = stack.enter_context(nc.sbuf_tensor("c1f", [3, 64], F32))[:]
    bwTf = stack.enter_context(nc.sbuf_tensor("bwTf", [3, 16], F32))[:]
    dcf = stack.enter_context(nc.sbuf_tensor("dcf", [64, 16], F32))[:]
    biasT = stack.enter_context(nc.sbuf_tensor("biasf", [128, 5 * NCH], F32))[:]
    a8f = stack.enter_context(nc.sbuf_tensor("a8f", [8, 128], F32))[:]
    aall = stack.enter_context(nc.sbuf_tensor("aall", [64, 16], F32))[:]
    atf = stack.enter_context(nc.sbuf_tensor("atf", [16, 64], F32))[:]
    m1b = stack.enter_context(nc.sbuf_tensor("m1b", [128, 128], BF16))[:]
    tmpb = stack.enter_context(nc.sbuf_tensor("tmpb", [128, 128], BF16))[:]
    gatef = stack.enter_context(nc.sbuf_tensor("gatef", [8, 1024], BF16))[:]
    scols = stack.enter_context(nc.sbuf_tensor("scols", [128, 8], F32))[:]
    PST = stack.enter_context(nc.psum_tensor("PS", [128, 4096], F32))

    BIG = BIGT[:]
    BIGF = BIG.bitcast(F32)
    WAR = WART[:]
    WARF = WAR.bitcast(F32)
    XST = XSTT[:]
    XSTB = XST.bitcast(BF16)
    PS = PST[:]

    p = Prog(nc, stack)
    SD = p.sem("SD")
    SG = p.sem("SG")
    SP = p.sem("SP")
    SV = p.sem("SV")
    SA = p.sem("SA")

    def w(s):
        return (s, s.v)

    def wv(s, v):
        return (s, v)

    def dma(eng, dst, src, sem, waits=()):
        return p.op(eng, (lambda d=dst, s=src: lambda e: e.dma_start(d, s))(),
                    waits=waits, incs=[(sem, 16)])

    def mm(out_ap, lhsT, rhs, start, stop, waits=(), incs=()):
        return p.eop("tensor", "matmul", out_ap, lhsT, rhs, start=start,
                     stop=stop, waits=waits, incs=incs)

    dma("sync", idn, idn_in, SD)
    dma("sync", c1f, c1, SD)
    dma("sync", bwTf, bwT, SD)
    dma("sync", dcf, dconst, SD)
    dma("sync", repf, repm, SD)
    dma("sync", biasT[:, 0 * NCH:1 * NCH], bqc, SD)
    dma("sync", biasT[:, 1 * NCH:2 * NCH], bkc, SD)
    dma("sync", biasT[:, 2 * NCH:3 * NCH], bvc, SD)
    dma("sync", biasT[:, 3 * NCH:4 * NCH], gamc, SD)
    dma("sync", biasT[:, 4 * NCH:5 * NCH], betc, SD)
    dma("gpsimd", maskb, maskm, SG)
    dma("gpsimd", ppb, ppm, SG)
    n_const = SD.v
    p.eop("vector", "memset", onesc, 1.0)
    p.eop("vector", "memset", onesr, 1.0, incs=[(SV, 1)])

    mm(PS[0:64, 0:16], c1f, bwTf, True, True, waits=[wv(SD, n_const)],
       incs=[(SP, 1)])
    p.eop("vector", "tensor_tensor", aall, PS[0:64, 0:16], dcf, ALU.add,
          waits=[w(SP)], incs=[(SV, 1)])
    p.eop("tensor", "transpose", PS[0:16, 512:576], aall, idn[0:64, 0:64],
          waits=[w(SV)], incs=[(SP, 1)])
    p.eop("vector", "tensor_copy", atf, PS[0:16, 512:576], waits=[w(SP)],
          incs=[(SV, 1)])
    dma("sync", adrm, atf, SD, waits=[w(SV)])
    dma("sync", a8f.rearrange("i (h j) -> i h j", h=16),
        adrm.rearrange("h (i j) -> i h j", i=8, j=8), SD, waits=[w(SD)])
    v_a8 = SD.v
    for h in range(NH):
        mm(PS[:, 1024 + h * 8:1024 + h * 8 + 8], repf, a8f[:, h * 8:h * 8 + 8],
           True, True,
           waits=[wv(SD, v_a8)] if h == 0 else (),
           incs=[(SP, 1)] if h == NH - 1 else ())
    p.eop("vector", "tensor_copy", m1b, PS[:, 1024:1152], waits=[w(SP)],
          incs=[(SV, 1)])
    for h in range(NH):
        for d in range(16):
            p.eop("vector", "tensor_copy", tmpb[:, d * 8:d * 8 + 8],
                  m1b[:, h * 8:h * 8 + 8],
                  waits=[wv(SG, 16)] if (h == 0 and d == 0) else ())
        p.eop("vector", "tensor_tensor", TBLK[:, h * 128:(h + 1) * 128], tmpb,
              maskb, ALU.mult, incs=[(SV, 1)] if h == NH - 1 else ())
    p.barrier()

    XBF = [BIG[:, 0:2048], BIG[:, 2048:4096]]
    XSQ = [BIG[:, 4096:6144], BIG[:, 6144:8192]]
    pe_p1, sq_p1 = {}, {}
    for fc in range(NCH):
        buf = fc % 2
        waits = [wv(SP, pe_p1[fc - 2]), wv(SV, sq_p1[fc - 2])] if fc >= 2 else []
        dma("gpsimd", XBF[buf], xfm[fc * 128:(fc + 1) * 128, :], SG, waits=waits)
        p.eop("vector", "tensor_tensor", XSQ[buf], XBF[buf], XBF[buf], ALU.mult,
              waits=[w(SG)], incs=[(SV, 1)])
        sq_p1[fc] = SV.v
        for kb in range(4):
            mm(PS[0:1, kb * 512:kb * 512 + 512], onesc,
               XBF[buf][:, kb * 512:kb * 512 + 512],
               fc == 0, fc == NCH - 1,
               waits=[w(SG)] if kb == 0 else ())
        for kb in range(4):
            mm(PS[0:1, 2048 + kb * 512:2048 + kb * 512 + 512], onesc,
               XSQ[buf][:, kb * 512:kb * 512 + 512],
               fc == 0, fc == NCH - 1,
               waits=[wv(SV, sq_p1[fc])] if kb == 0 else (),
               incs=[(SP, 1)] if kb == 3 else ())
        pe_p1[fc] = SP.v
    R0 = WARF[0:1, 0:2048]
    R1 = WARF[0:1, 2048:4096]
    MH = WAR[0:1, 8192:10240]
    RH = WAR[0:1, 10240:12288]
    RL = WAR[0:1, 12288:14336]
    pe_all = SP.v
    p.eop("vector", "tensor_scalar_mul", R0, PS[0:1, 0:2048], 1.0 / DM,
          waits=[wv(SP, pe_all)])
    p.eop("vector", "tensor_tensor", R1, R0, R0, ALU.mult)
    p.eop("vector", "scalar_tensor_tensor", R1, PS[0:1, 2048:4096], 1.0 / DM,
          R1, ALU.mult, ALU.subtract, incs=[(SV, 1)])
    v_var = SV.v
    p.eop("vector", "tensor_scalar_add", R1, R1, EPS, incs=[(SV, 1)])
    v_vare = SV.v
    p.eop("scalar", "activation", R1, R1, AF.Ln, waits=[wv(SV, v_vare)],
          incs=[(SA, 1)])
    p.eop("vector", "tensor_scalar_mul", R1, R1, -0.5, waits=[w(SA)],
          incs=[(SV, 1)])
    p.eop("scalar", "activation", R1, R1, AF.Exp, waits=[w(SV)], incs=[(SA, 1)])
    a_rstd = SA.v
    p.eop("vector", "tensor_copy", MH, R0, waits=[wv(SA, a_rstd)])
    p.eop("vector", "tensor_copy", RH, R1)
    p.eop("vector", "tensor_copy", R0, RH)
    p.eop("vector", "tensor_tensor", RL, R1, R0, ALU.subtract, incs=[(SV, 1)])
    p.barrier()

    def ln1c(fc):
        return BIG[:, fc * 2048:(fc + 1) * 2048]

    for kb in range(4):
        mm(PS[:, kb * 512:kb * 512 + 512], onesr, MH[:, kb * 512:kb * 512 + 512],
           True, True)
    for kb in range(4):
        mm(PS[:, 2048 + kb * 512:2048 + kb * 512 + 512], onesr,
           RH[:, kb * 512:kb * 512 + 512], True, False)
        mm(PS[:, 2048 + kb * 512:2048 + kb * 512 + 512], onesr,
           RL[:, kb * 512:kb * 512 + 512], False, True,
           incs=[(SP, 1)] if kb == 3 else ())
    pe_bc = SP.v
    XC = [XST[:, 0:2048], XST[:, 2048:4096]]
    KSLAB = [WAR[:, 0:8192], WAR[:, 8192:16384]]
    dma("gpsimd", KSLAB[0].rearrange("q (c f) -> q c f", c=NCH),
        wkT[:, 0:256].rearrange("(c q) f -> q c f", q=128),
        SG, waits=[wv(SP, pe_bc)])
    sg_pf_k = SG.v
    v_ln1 = {}
    for fc in range(NCH):
        buf = fc % 2
        waits = [wv(SV, v_ln1[fc - 2])] if fc >= 2 else []
        dma("sync", XC[buf], xfm[fc * 128:(fc + 1) * 128, :], SD, waits=waits)
        p.eop("vector", "tensor_tensor", XC[buf], XC[buf], PS[:, 0:2048],
              ALU.subtract,
              waits=[w(SD)] + ([wv(SP, pe_bc)] if fc == 0 else []))
        p.eop("vector", "tensor_tensor", ln1c(fc), XC[buf], PS[:, 2048:4096],
              ALU.mult, incs=[(SV, 1)])
        v_ln1[fc] = SV.v
    p.barrier()

    KSTG = [XSTB[:, 0:2048], XSTB[:, 2048:4096]]
    sp_slab, sa_kst, sd_kst = {}, {}, {}
    for pr in range(16):
        buf = pr % 2
        if pr == 0:
            sg_s = sg_pf_k
        else:
            waits = [wv(SP, sp_slab[pr - 2])] if pr >= 2 else []
            dma("gpsimd", KSLAB[buf].rearrange("q (c f) -> q c f", c=NCH),
                wkT[:, pr * 256:(pr + 1) * 256].rearrange("(c q) f -> q c f", q=128),
                SG, waits=waits)
            sg_s = SG.v
        for fc in range(NCH):
            for fpc in range(2):
                for kb in range(4):
                    first = fc == 0 and fpc == 0 and kb == 0
                    mm(PS[:, fpc * 2048 + kb * 512:fpc * 2048 + kb * 512 + 512],
                       KSLAB[buf][:, fc * 256 + fpc * 128:fc * 256 + fpc * 128 + 128],
                       ln1c(fc)[:, kb * 512:kb * 512 + 512],
                       fc == 0, fc == NCH - 1,
                       waits=(([wv(SG, sg_s)] if first else []) +
                              ([wv(SA, sa_kst[pr - 1])] if (pr >= 1 and first) else [])),
                       incs=[(SP, 1)] if (fc == NCH - 1 and fpc == 1 and kb == 3) else ())
        sp_slab[pr] = SP.v
        for fpc in range(2):
            p.eop("scalar", "activation", KSTG[fpc],
                  PS[:, fpc * 2048:fpc * 2048 + 2048], AF.Identity,
                  bias=biasT[:, 1 * NCH + 2 * pr + fpc:1 * NCH + 2 * pr + fpc + 1],
                  waits=[wv(SP, sp_slab[pr])] +
                        ([wv(SD, sd_kst[pr - 1])] if (pr >= 1 and fpc == 0) else []),
                  incs=[(SA, 1)])
            dma("sync", ksp[(2 * pr + fpc) * 128:(2 * pr + fpc + 1) * 128, :],
                KSTG[fpc], SD, waits=[w(SA)])
        sa_kst[pr] = SA.v
        sd_kst[pr] = SD.v
    VSLAB = [WAR[:, 0:8192], WAR[:, 8192:16384]]
    dma("gpsimd", VSLAB[0].rearrange("q (c f) -> q c f", c=NCH),
        wvT[:, 0:256].rearrange("(c q) f -> q c f", q=128),
        SG, waits=[wv(SP, sp_slab[14])])
    sg_pf_v = SG.v
    p.barrier()

    VSTG = XSTB[:, 0:2048]
    sv_vst, sd_vst = {}, {}
    sp_vslab = {}
    for vb in range(16):
        buf = vb % 2
        if vb == 0:
            sg_s = sg_pf_v
        else:
            waits = [wv(SP, sp_vslab[vb - 2])] if vb >= 2 else []
            dma("gpsimd", VSLAB[buf].rearrange("q (c f) -> q c f", c=NCH),
                wvT[:, vb * 256:(vb + 1) * 256].rearrange("(c q) f -> q c f", q=128),
                SG, waits=waits)
            sg_s = SG.v
        for th in range(2):
            prev = (vb, th - 1) if th else (vb - 1, 1)
            for fc in range(NCH):
                for tt in range(8):
                    first = fc == 0 and tt == 0
                    mm(PS[:, tt * 512:tt * 512 + 256],
                       ln1c(fc)[:, (th * 8 + tt) * 128:(th * 8 + tt + 1) * 128],
                       VSLAB[buf][:, fc * 256:(fc + 1) * 256],
                       fc == 0, fc == NCH - 1,
                       waits=(([wv(SG, sg_s)] if (first and th == 0) else []) +
                              ([wv(SV, sv_vst[prev])]
                               if (first and prev in sv_vst) else [])),
                       incs=[(SP, 1)] if (fc == NCH - 1 and tt == 7) else ())
            sp_v = SP.v
            sp_vslab[vb] = sp_v
            for tt in range(8):
                p.eop("vector", "tensor_copy", VSTG[:, tt * 256:(tt + 1) * 256],
                      PS[:, tt * 512:tt * 512 + 256],
                      waits=(([wv(SP, sp_v)] if tt == 0 else []) +
                             ([wv(SD, sd_vst[prev])]
                              if (tt == 0 and prev in sd_vst) else [])),
                      incs=[(SV, 1)] if tt == 7 else ())
            sv_vst[(vb, th)] = SV.v
            dma("sync",
                vsp[th * 1024:(th + 1) * 1024, vb * 256:(vb + 1) * 256]
                .rearrange("(tt q) f -> q tt f", q=128),
                VSTG.rearrange("q (tt f) -> q tt f", tt=8), SD, waits=[w(SV)])
            sd_vst[(vb, th)] = SD.v
    QSLAB = [WAR[:, 0:8192], WAR[:, 8192:16384]]
    dma("gpsimd", QSLAB[0].rearrange("q (c f) -> q c f", c=NCH),
        wqT[:, 0:256].rearrange("(c q) f -> q c f", q=128),
        SG, waits=[wv(SP, sp_vslab[14])])
    sg_pf_q = SG.v
    p.barrier()

    QTMP = XSTB[:, 0:2048]
    QSTG = XSTB[:, 2048:4096]
    sp_qslab, sa_qt, sp_qm, sv_qst, sd_qst = {}, {}, {}, {}, {}
    for pr in range(16):
        buf = pr % 2
        if pr == 0:
            sg_s = sg_pf_q
        else:
            waits = [wv(SP, sp_qslab[pr - 2])] if pr >= 2 else []
            dma("gpsimd", QSLAB[buf].rearrange("q (c f) -> q c f", c=NCH),
                wqT[:, pr * 256:(pr + 1) * 256].rearrange("(c q) f -> q c f", q=128),
                SG, waits=waits)
            sg_s = SG.v
        for fc in range(NCH):
            for fpc in range(2):
                for qb in range(2):
                    first = fc == 0 and fpc == 0 and qb == 0
                    mm(PS[:, fpc * 1024 + qb * 512:fpc * 1024 + qb * 512 + 512],
                       QSLAB[buf][:, fc * 256 + fpc * 128:fc * 256 + fpc * 128 + 128],
                       ln1c(fc)[:, qb * 512:qb * 512 + 512],
                       fc == 0, fc == NCH - 1,
                       waits=(([wv(SG, sg_s)] if first else []) +
                              ([wv(SA, sa_qt[pr - 1])] if (pr >= 1 and first) else [])),
                       incs=[(SP, 1)] if (fc == NCH - 1 and fpc == 1 and qb == 1) else ())
        sp_q = SP.v
        sp_qslab[pr] = sp_q
        for fpc in range(2):
            p.eop("scalar", "activation", QTMP[:, fpc * 1024:(fpc + 1) * 1024],
                  PS[:, fpc * 1024:fpc * 1024 + 1024], AF.Identity,
                  bias=biasT[:, 0 * NCH + 2 * pr + fpc:0 * NCH + 2 * pr + fpc + 1],
                  waits=[wv(SP, sp_q)] +
                        ([wv(SP, sp_qm[pr - 1])] if (pr >= 1 and fpc == 0) else []),
                  incs=[(SA, 1)])
        sa_qt[pr] = SA.v
        for fpc in range(2):
            for qb in range(2):
                first = fpc == 0 and qb == 0
                mm(PS[:, 2048 + fpc * 1024 + qb * 512:
                       2048 + fpc * 1024 + qb * 512 + 512],
                   TBLK[:, pr * 128:(pr + 1) * 128],
                   QTMP[:, fpc * 1024 + qb * 512:fpc * 1024 + qb * 512 + 512],
                   True, True,
                   waits=(([wv(SA, sa_qt[pr])] if first else []) +
                          ([wv(SV, sv_qst[pr - 1])] if (pr >= 1 and first) else [])),
                   incs=[(SP, 1)] if (fpc == 1 and qb == 1) else ())
        sp_qm[pr] = SP.v
        for fpc in range(2):
            p.eop("vector", "tensor_copy", QSTG[:, fpc * 1024:(fpc + 1) * 1024],
                  PS[:, 2048 + fpc * 1024:2048 + fpc * 1024 + 1024],
                  waits=([wv(SP, sp_qm[pr])] if fpc == 0 else []) +
                        ([wv(SD, sd_qst[pr - 1])] if (pr >= 1 and fpc == 0) else []),
                  incs=[(SV, 1)] if fpc == 1 else ())
        sv_qst[pr] = SV.v
        dma("sync",
            qsp[pr * 256:(pr + 1) * 256, :].rearrange("(c q) t -> q c t", q=128),
            QSTG.rearrange("q (c t) -> q c t", c=2), SD, waits=[w(SV)])
        sd_qst[pr] = SD.v
    p.barrier()

    E = BIG[:, 0:16384]

    def kvq(b):
        base = 16384 + b * 10240
        return (BIG[:, base:base + 4096],
                BIG[:, base + 4096:base + 8192],
                BIG[:, base + 8192:base + 10240])

    KSQ = WAR[:, 0:4096]
    QSQ = WAR[:, 4096:6144]
    RSK = WARF[0:1, 3072:3073]
    RM = WARF[0:1, 3584:4608]
    NM1 = WAR[0:1, 13312:13313]
    NEGC = BIGF[:, 18432:18433]
    RBC = BIGF[:, 18560:19584]
    T1F = BIGF[:, 19584:20608]
    XBF2 = BIGF[:, 20608:21632]
    RROW = BIGF[0:1, 21760:22784]
    RHI = BIG[0:1, 45568:46592]
    RLO = BIG[0:1, 46592:47616]
    X2STG = XST[:, 0:2048]
    XRC = XST[:, 2048:4096]
    sd_kvq, sp_head, sa_head, sv_head, sd_x2, sd_xr = {}, {}, {}, {}, {}, {}
    for h in range(NH):
        buf = h % 2
        Kh, Vh, Qm = kvq(buf)
        waits = [wv(SP, sp_head[h - 2]), wv(SV, sv_head[h - 2])] if h >= 2 else []
        dma("sync", Kh.rearrange("q (c t) -> q c t", c=2),
            ksp[h * 256:(h + 1) * 256, :].rearrange("(c q) t -> q c t", q=128),
            SD, waits=waits)
        dma("sync", Vh.rearrange("q (kc f) -> q kc f", kc=16),
            vsp[:, h * 256:(h + 1) * 256].rearrange("(kc q) f -> q kc f", q=128),
            SD)
        dma("sync", Qm.rearrange("q (c t) -> q c t", c=2),
            qsp[h * 256:(h + 1) * 256, :].rearrange("(c q) t -> q c t", q=128),
            SD)
        sd_kvq[h] = SD.v
        dma("sync", XRC.rearrange("q (c t) -> q c t", c=2),
            xfm[h * 256:(h + 1) * 256, 0:LOWN].rearrange("(c q) t -> q c t", q=128),
            SD, waits=[wv(SA, sa_head[h - 1])] if h >= 1 else [])
        sd_xr[h] = SD.v
        p.eop("vector", "tensor_tensor", KSQ, Kh, Kh, ALU.mult,
              waits=[wv(SD, sd_kvq[h])] +
                    ([wv(SP, sp_normh[h - 1])] if h >= 1 else []))
        p.eop("vector", "tensor_tensor", QSQ, Qm, Qm, ALU.mult, incs=[(SV, 1)])
        sv_sq = SV.v
        for fc in range(2):
            for kb in range(4):
                first = fc == 0 and kb == 0
                mm(PS[0:1, kb * 512:kb * 512 + 512], onesc,
                   KSQ[:, fc * 2048 + kb * 512:fc * 2048 + kb * 512 + 512],
                   fc == 0, fc == 1,
                   waits=(([wv(SV, sv_sq)] if first else []) +
                          ([wv(SV, v_rbch[h - 1])] if (h >= 1 and first) else [])))
        for fc in range(2):
            for qb in range(2):
                mm(PS[0:1, 2048 + qb * 512:2048 + qb * 512 + 512], onesc,
                   QSQ[:, fc * 1024 + qb * 512:fc * 1024 + qb * 512 + 512],
                   fc == 0, fc == 1,
                   waits=([wv(SV, v_ep0h[h - 1])]
                          if (h >= 1 and fc == 0 and qb == 0) else []),
                   incs=[(SP, 1)] if (fc == 1 and qb == 1) else ())
        sp_norm = SP.v
        if h == 0:
            sp_normh, v_rbch, v_ep0h, v_ep1h = {}, {}, {}, {}
        sp_normh[h] = sp_norm
        if dbg4 is not None and h == 0:
            p.eop("vector", "tensor_copy", WARF[0:1, 7168:8192],
                  PS[0:1, 2048:3072], waits=[wv(SP, sp_norm)], incs=[(SV, 1)])
            dma("sync", dbg4[0:1, 2304:3328], WARF[0:1, 7168:8192], SD,
                waits=[w(SV)])
            dma("sync", dbg5[:, 0:1024].bitcast(BF16), QSQ, SD)
            dma("sync", dbg5[:, 1024:2048].bitcast(BF16), Qm, SD)
            dma("sync", dbg5[:, 2048:3072].bitcast(BF16), KSQ[:, 0:2048], SD)
            dma("sync", dbg5[:, 3072:3584].bitcast(BF16), Kh[:, 0:1024], SD)
        p.eop("vector", "reduce_max", RSK, PS[0:1, 0:2048], axis=AX,
              waits=[wv(SP, sp_norm)])
        p.eop("vector", "tensor_copy", RM, PS[0:1, 2048:3072])
        p.eop("vector", "tensor_single_scalar", RM, RM, RSK,
              ALU.mult, incs=[(SV, 1)])
        v_rm = SV.v
        p.eop("scalar", "activation", RM, RM, AF.Ln, waits=[wv(SV, v_rm)],
              incs=[(SA, 1)])
        p.eop("vector", "tensor_scalar_mul", RM, RM, 0.5, waits=[w(SA)],
              incs=[(SV, 1)])
        p.eop("scalar", "activation", RM, RM, AF.Exp, waits=[w(SV)],
              incs=[(SA, 1)])
        a_rm = SA.v
        p.eop("vector", "reduce_max", NM1, RM, axis=AX, negate=True,
              waits=[wv(SA, a_rm)], incs=[(SV, 1)])
        v_nm = SV.v
        mm(PS[:, 3072:3073], onesr, NM1, True, True,
           waits=[wv(SV, v_nm)] +
                 ([wv(SV, v_ep1h[h - 1])] if h >= 1 else []),
           incs=[(SP, 1)])
        sp_nm = SP.v
        p.eop("vector", "tensor_copy", NEGC, PS[:, 3072:3073],
              waits=[wv(SP, sp_nm)], incs=[(SV, 1)])
        v_negc = SV.v
        sa_exp, sp_L, sp_av = {}, {}, {}
        for kc in range(16):
            lb = kc % 2
            for qb in range(2):
                for fc in range(2):
                    first = qb == 0 and fc == 0
                    mm(PS[:, lb * 1024 + qb * 512:lb * 1024 + qb * 512 + 512],
                       Kh[:, fc * 2048 + kc * 128:fc * 2048 + kc * 128 + 128],
                       Qm[:, fc * 1024 + qb * 512:fc * 1024 + qb * 512 + 512],
                       fc == 0, fc == 1,
                       waits=[wv(SA, sa_exp[kc - 2])] if (kc >= 2 and first) else [],
                       incs=[(SP, 1)] if (qb == 1 and fc == 1) else ())
            sp_L[kc] = SP.v
            p.eop("scalar", "activation", E[:, kc * 1024:(kc + 1) * 1024],
                  PS[:, lb * 1024:lb * 1024 + 1024], AF.Exp, bias=NEGC,
                  waits=[wv(SP, sp_L[kc]), wv(SV, v_negc)] +
                        ([wv(SP, sp_av[kc - 1])] if kc >= 1 else []),
                  incs=[(SA, 1)])
            sa_exp[kc] = SA.v
            for ft in range(2):
                for qb in range(2):
                    mm(PS[:, 2048 + ft * 1024 + qb * 512:
                           2048 + ft * 1024 + qb * 512 + 512],
                       Vh[:, kc * 256 + ft * 128:kc * 256 + ft * 128 + 128],
                       E[:, kc * 1024 + qb * 512:kc * 1024 + qb * 512 + 512],
                       kc == 0, kc == 15,
                       waits=[wv(SA, sa_exp[kc])] if (ft == 0 and qb == 0) else [],
                       incs=[(SP, 1)] if (ft == 1 and qb == 1) else ())
            sp_av[kc] = SP.v
        for kc in range(16):
            for qb in range(2):
                mm(PS[0:1, qb * 512:qb * 512 + 512], onesc,
                   E[:, kc * 1024 + qb * 512:kc * 1024 + qb * 512 + 512],
                   kc == 0, kc == 15,
                   waits=[wv(SA, sa_exp[15])] if (kc == 0 and qb == 0) else [],
                   incs=[(SP, 1)] if (kc == 15 and qb == 1) else ())
        sp_den = SP.v
        p.eop("vector", "reciprocal", RROW, PS[0:1, 0:1024],
              waits=[wv(SP, sp_den)])
        p.eop("vector", "tensor_copy", RHI, RROW)
        p.eop("vector", "tensor_copy", T1F[0:1, 0:1024], RHI)
        p.eop("vector", "tensor_tensor", RLO, RROW, T1F[0:1, 0:1024],
              ALU.subtract, incs=[(SV, 1)])
        v_rlo = SV.v
        for qb in range(2):
            mm(PS[:, qb * 512:qb * 512 + 512], onesr,
               RHI[:, qb * 512:qb * 512 + 512], True, False,
               waits=[wv(SV, v_rlo)] if qb == 0 else [])
            mm(PS[:, qb * 512:qb * 512 + 512], onesr,
               RLO[:, qb * 512:qb * 512 + 512], False, True,
               incs=[(SP, 1)] if qb == 1 else ())
        sp_rbc = SP.v
        p.eop("vector", "tensor_copy", RBC, PS[:, 0:1024],
              waits=[wv(SP, sp_rbc)], incs=[(SV, 1)])
        v_rbc = SV.v
        v_rbch[h] = v_rbc
        if dbg4 is not None and h == 0:
            dma("sync", dbg4[0:1, 0:1024], RM, SD, waits=[w(SV)])
            dma("sync", dbg4[0:1, 1024:2048], RROW, SD)
            dma("sync", dbg4[0:1, 2048:2049], RSK, SD)
            p.op("sync", (lambda d=dbg4[:, 2049:2050], s=NEGC:
                          lambda e: e.dma_start(d, s,
                                                allow_slow_non_contiguous=True))(),
                 incs=[(SD, 16)])
            dma("sync", dbg5[:, 3584:4096].bitcast(BF16),
                E[:, 0:1024], SD)
            for _e in ("vector", "scalar", "tensor", "gpsimd"):
                p.op(_e, None, waits=[wv(SD, SD.v)])
        for ft in range(2):
            p.eop("scalar", "activation", XBF2, XRC[:, ft * 1024:(ft + 1) * 1024],
                  AF.Identity,
                  bias=biasT[:, 2 * NCH + 2 * h + ft:2 * NCH + 2 * h + ft + 1],
                  waits=[wv(SD, sd_xr[h])] + ([wv(SV, SV.v)] if ft == 1 else []),
                  incs=[(SA, 1)])
            a_xb = SA.v
            p.eop("vector", "tensor_tensor", T1F,
                  PS[:, 2048 + ft * 1024:2048 + ft * 1024 + 1024], RBC, ALU.mult,
                  waits=[wv(SV, v_rbc), wv(SA, a_xb)] +
                        ([wv(SD, sd_x2[h - 1])] if (h >= 1 and ft == 0) else []),
                  incs=[(SV, 1)])
            if ft == 0:
                v_ep0h[h] = SV.v
            else:
                v_ep1h[h] = SV.v
            p.eop("vector", "tensor_tensor", X2STG[:, ft * 1024:(ft + 1) * 1024],
                  T1F, XBF2, ALU.add, incs=[(SV, 1)])
        sv_head[h] = SV.v
        dma("sync",
            x2sp[h * 256:(h + 1) * 256, :].rearrange("(c q) t -> q c t", q=128),
            X2STG.rearrange("q (c t) -> q c t", c=2), SD, waits=[w(SV)])
        sd_x2[h] = SD.v
        sp_head[h] = SP.v
        sa_head[h] = SA.v
    p.barrier()

    def ln2c(fc):
        return BIG[:, fc * 1024:(fc + 1) * 1024]

    X2B = [BIG[:, 32768:33792], BIG[:, 33792:34816]]
    X2Q = [BIG[:, 34816:35840], BIG[:, 35840:36864]]
    pe_p5, sq_p5 = {}, {}
    for fc in range(NCH):
        buf = fc % 2
        waits = [wv(SP, pe_p5[fc - 2]), wv(SV, sq_p5[fc - 2])] if fc >= 2 else []
        dma("gpsimd", X2B[buf], x2sp[fc * 128:(fc + 1) * 128, :], SG, waits=waits)
        p.eop("vector", "tensor_tensor", X2Q[buf], X2B[buf], X2B[buf], ALU.mult,
              waits=[w(SG)], incs=[(SV, 1)])
        sq_p5[fc] = SV.v
        for qb in range(2):
            mm(PS[0:1, qb * 512:qb * 512 + 512], onesc,
               X2B[buf][:, qb * 512:qb * 512 + 512], fc == 0, fc == NCH - 1,
               waits=[w(SG)] if qb == 0 else ())
        for qb in range(2):
            mm(PS[0:1, 1024 + qb * 512:1024 + qb * 512 + 512], onesc,
               X2Q[buf][:, qb * 512:qb * 512 + 512], fc == 0, fc == NCH - 1,
               waits=[wv(SV, sq_p5[fc])] if qb == 0 else (),
               incs=[(SP, 1)] if qb == 1 else ())
        pe_p5[fc] = SP.v
    R0 = WARF[0:1, 0:1024]
    R1 = WARF[0:1, 1024:2048]
    MH2 = WAR[0:1, 4096:5120]
    RH2 = WAR[0:1, 5120:6144]
    RL2 = WAR[0:1, 6144:7168]
    pe_all2 = SP.v
    p.eop("vector", "tensor_scalar_mul", R0, PS[0:1, 0:1024], 1.0 / DM,
          waits=[wv(SP, pe_all2)])
    p.eop("vector", "tensor_tensor", R1, R0, R0, ALU.mult)
    p.eop("vector", "scalar_tensor_tensor", R1, PS[0:1, 1024:2048], 1.0 / DM,
          R1, ALU.mult, ALU.subtract, incs=[(SV, 1)])
    v_var2 = SV.v
    p.eop("vector", "tensor_scalar_add", R1, R1, EPS, incs=[(SV, 1)])
    v_vare2 = SV.v
    p.eop("scalar", "activation", R1, R1, AF.Ln, waits=[wv(SV, v_vare2)],
          incs=[(SA, 1)])
    p.eop("vector", "tensor_scalar_mul", R1, R1, -0.5, waits=[w(SA)],
          incs=[(SV, 1)])
    p.eop("scalar", "activation", R1, R1, AF.Exp, waits=[w(SV)], incs=[(SA, 1)])
    a_rstd2 = SA.v
    p.eop("vector", "tensor_copy", MH2, R0, waits=[wv(SA, a_rstd2)])
    p.eop("vector", "tensor_copy", RH2, R1)
    p.eop("vector", "tensor_copy", R0, RH2)
    p.eop("vector", "tensor_tensor", RL2, R1, R0, ALU.subtract, incs=[(SV, 1)])
    p.barrier()

    for qb in range(2):
        mm(PS[:, qb * 512:qb * 512 + 512], onesr,
           MH2[:, qb * 512:qb * 512 + 512], True, True)
    for qb in range(2):
        mm(PS[:, 1024 + qb * 512:1024 + qb * 512 + 512], onesr,
           RH2[:, qb * 512:qb * 512 + 512], True, False)
        mm(PS[:, 1024 + qb * 512:1024 + qb * 512 + 512], onesr,
           RL2[:, qb * 512:qb * 512 + 512], False, True,
           incs=[(SP, 1)] if qb == 1 else ())
    pe_bc2 = SP.v
    X2C = [XST[:, 0:1024], XST[:, 1024:2048]]
    TMPF2 = XST[:, 2048:3072]
    v_ln2 = {}
    for fc in range(NCH):
        buf = fc % 2
        waits = [wv(SV, v_ln2[fc - 2])] if fc >= 2 else []
        dma("sync", X2C[buf], x2sp[fc * 128:(fc + 1) * 128, :], SD, waits=waits)
        p.eop("vector", "tensor_tensor", TMPF2, X2C[buf], PS[:, 0:1024],
              ALU.subtract,
              waits=[w(SD)] + ([wv(SP, pe_bc2)] if fc == 0 else []))
        p.eop("vector", "tensor_tensor", ln2c(fc), TMPF2, PS[:, 1024:2048],
              ALU.mult, incs=[(SV, 1)])
        v_ln2[fc] = SV.v
    p.barrier()

    GW = WAR[:, 0:256]
    SBC = XST[:, 0:1024]
    SROW = WARF[0:1, 1024:2048]
    SHI = WAR[0:1, 4096:5120]
    SLO = WAR[0:1, 5120:6144]
    STMP = WARF[0:1, 512:1024]
    dma("gpsimd", GW.rearrange("q (c e) -> q c e", c=NCH),
        gw.rearrange("(c q) e -> q c e", q=128), SG)
    sg_gw = SG.v
    for fc in range(NCH):
        for qb in range(2):
            mm(PS[0:8, qb * 512:qb * 512 + 512], GW[:, fc * 8:(fc + 1) * 8],
               ln2c(fc)[:, qb * 512:qb * 512 + 512], fc == 0, fc == NCH - 1,
               waits=[wv(SG, sg_gw)] if (fc == 0 and qb == 0) else [],
               incs=[(SP, 1)] if (fc == NCH - 1 and qb == 1) else ())
    sp_g = SP.v
    p.eop("vector", "tensor_copy", gatef, PS[0:8, 0:1024], waits=[wv(SP, sp_g)],
          incs=[(SV, 1)])
    v_gf = SV.v
    v_sc = {}
    for tt in range(8):
        pb = 4 + tt % 2
        mm(PS[:, pb * 512:pb * 512 + 28], gatef[:, tt * 128:(tt + 1) * 128],
           ppb, True, True,
           waits=([wv(SV, v_gf)] if tt == 0 else []) +
                 ([wv(SV, v_sc[tt - 2])] if tt >= 2 else []),
           incs=[(SP, 1)])
        sp_pp = SP.v
        p.eop("vector", "reduce_max", scols[:, tt:tt + 1],
              PS[:, pb * 512:pb * 512 + 28], axis=AX,
              waits=[wv(SP, sp_pp)], incs=[(SV, 1)])
        v_sc[tt] = SV.v
    for tt in range(8):
        p.eop("tensor", "transpose",
              PS[0:1, 3072 + tt * 128:3072 + (tt + 1) * 128],
              scols[:, tt:tt + 1], idn,
              waits=[wv(SV, v_sc[7])] if tt == 0 else (),
              incs=[(SP, 1)] if tt == 7 else ())
    sp_tr = SP.v
    p.eop("vector", "tensor_copy", SROW, PS[0:1, 3072:4096],
          waits=[wv(SP, sp_tr)])
    p.eop("vector", "tensor_copy", SHI, SROW)
    p.eop("vector", "tensor_copy", WARF[0:1, 0:512], SHI[:, 0:512])
    p.eop("vector", "tensor_copy", STMP, SHI[:, 512:1024])
    p.eop("vector", "tensor_tensor", SLO[:, 0:512], SROW[:, 0:512],
          WARF[0:1, 0:512], ALU.subtract)
    p.eop("vector", "tensor_tensor", SLO[:, 512:1024], SROW[:, 512:1024],
          STMP, ALU.subtract, incs=[(SV, 1)])
    v_slo = SV.v
    for qb in range(2):
        mm(PS[:, qb * 512:qb * 512 + 512], onesr,
           SHI[:, qb * 512:qb * 512 + 512], True, False,
           waits=[wv(SV, v_slo)] if qb == 0 else [])
        mm(PS[:, qb * 512:qb * 512 + 512], onesr,
           SLO[:, qb * 512:qb * 512 + 512], False, True,
           incs=[(SP, 1)] if qb == 1 else ())
    sp_sbc = SP.v
    p.eop("vector", "tensor_copy", SBC, PS[:, 0:1024], waits=[wv(SP, sp_sbc)],
          incs=[(SV, 1)])
    F1S = [WAR[:, 0:8192], WAR[:, 8192:16384]]
    dma("gpsimd", F1S[0].rearrange("q (c f) -> q c f", c=NCH),
        f1w[:, 0:256].rearrange("(c q) f -> q c f", q=128),
        SG, waits=[wv(SP, sp_sbc)])
    sg_pf_f1 = SG.v
    p.barrier()

    LATS = BIG[:, 32768:49152]
    sp_f1, sv_lat = {}, {}
    for lp in range(8):
        buf = lp % 2
        if lp == 0:
            sg_s = sg_pf_f1
        else:
            waits = [wv(SP, sp_f1[lp - 2])] if lp >= 2 else []
            dma("gpsimd", F1S[buf].rearrange("q (c f) -> q c f", c=NCH),
                f1w[:, lp * 256:(lp + 1) * 256].rearrange("(c q) f -> q c f", q=128),
                SG, waits=waits)
            sg_s = SG.v
        for lc2 in range(2):
            for fc in range(NCH):
                for qb in range(2):
                    first = lc2 == 0 and fc == 0 and qb == 0
                    mm(PS[:, lc2 * 1024 + qb * 512:lc2 * 1024 + qb * 512 + 512],
                       F1S[buf][:, fc * 256 + lc2 * 128:fc * 256 + lc2 * 128 + 128],
                       ln2c(fc)[:, qb * 512:qb * 512 + 512],
                       fc == 0, fc == NCH - 1,
                       waits=(([wv(SG, sg_s)] if first else []) +
                              ([wv(SV, sv_lat[lp - 1])] if (lp >= 1 and first) else [])),
                       incs=[(SP, 1)] if (fc == NCH - 1 and qb == 1) else ())
        sp_f1[lp] = SP.v
        for lc2 in range(2):
            p.eop("vector", "tensor_tensor",
                  LATS[:, (lp * 2 + lc2) * 1024:(lp * 2 + lc2 + 1) * 1024],
                  PS[:, lc2 * 1024:lc2 * 1024 + 1024], SBC, ALU.mult,
                  waits=[wv(SP, sp_f1[lp])] if lc2 == 0 else [],
                  incs=[(SV, 1)] if lc2 == 1 else ())
        sv_lat[lp] = SV.v
    EWS = [WAR[:, 0:4096], WAR[:, 4096:8192]]
    dma("gpsimd", EWS[0].rearrange("q (c f) -> q c f", c=16),
        ew[:, 0:256].rearrange("(c q) f -> q c f", q=128),
        SG, waits=[wv(SP, sp_f1[6])])
    sg_pf_ew = SG.v
    p.barrier()

    EXO = BIG[:, 49152:65536]
    sp_ew, sv_exo = {}, {}
    for lp in range(8):
        buf = lp % 2
        if lp == 0:
            sg_s = sg_pf_ew
        else:
            waits = [wv(SP, sp_ew[lp - 2])] if lp >= 2 else []
            dma("gpsimd", EWS[buf].rearrange("q (c f) -> q c f", c=16),
                ew[:, lp * 256:(lp + 1) * 256].rearrange("(c q) f -> q c f", q=128),
                SG, waits=waits)
            sg_s = SG.v
        for lc2 in range(2):
            for lc in range(16):
                for qb in range(2):
                    first = lc2 == 0 and lc == 0 and qb == 0
                    mm(PS[:, 2048 + lc2 * 1024 + qb * 512:
                           2048 + lc2 * 1024 + qb * 512 + 512],
                       EWS[buf][:, lc * 256 + lc2 * 128:lc * 256 + lc2 * 128 + 128],
                       LATS[:, lc * 1024 + qb * 512:lc * 1024 + qb * 512 + 512],
                       lc == 0, lc == 15,
                       waits=(([wv(SG, sg_s)] if first else []) +
                              ([wv(SV, sv_exo[lp - 1])] if (lp >= 1 and first) else [])),
                       incs=[(SP, 1)] if (lc == 15 and qb == 1) else ())
        sp_ew[lp] = SP.v
        for lc2 in range(2):
            p.eop("vector", "tensor_copy",
                  EXO[:, (lp * 2 + lc2) * 1024:(lp * 2 + lc2 + 1) * 1024],
                  PS[:, 2048 + lc2 * 1024:2048 + lc2 * 1024 + 1024],
                  waits=[wv(SP, sp_ew[lp])] if lc2 == 0 else [],
                  incs=[(SV, 1)] if lc2 == 1 else ())
        sv_exo[lp] = SV.v
    F2S = [WAR[:, 0:2048], WAR[:, 2048:4096]]
    dma("gpsimd", F2S[0].rearrange("q (c f) -> q c f", c=16),
        f2w[:, 0:128].rearrange("(c q) f -> q c f", q=128),
        SG, waits=[wv(SP, sp_ew[6])])
    sg_pf_f2 = SG.v
    p.barrier()

    PSTG = [XST[:, 0:1024], XST[:, 1024:2048]]
    sp_f2, sv_part, sd_part = {}, {}, {}
    for hc in range(NCH):
        buf = hc % 2
        if hc == 0:
            sg_s = sg_pf_f2
        else:
            waits = [wv(SP, sp_f2[hc - 2])] if hc >= 2 else []
            dma("gpsimd", F2S[buf].rearrange("q (c f) -> q c f", c=16),
                f2w[:, hc * 128:(hc + 1) * 128].rearrange("(c q) f -> q c f", q=128),
                SG, waits=waits)
            sg_s = SG.v
        pbk = hc % 2
        for lc in range(16):
            for qb in range(2):
                first = lc == 0 and qb == 0
                mm(PS[:, pbk * 1024 + qb * 512:pbk * 1024 + qb * 512 + 512],
                   F2S[buf][:, lc * 128:(lc + 1) * 128],
                   EXO[:, lc * 1024 + qb * 512:lc * 1024 + qb * 512 + 512],
                   lc == 0, lc == 15,
                   waits=(([wv(SG, sg_s)] if first else []) +
                          ([wv(SV, sv_part[hc - 2])] if (hc >= 2 and first) else [])),
                   incs=[(SP, 1)] if (lc == 15 and qb == 1) else ())
        sp_f2[hc] = SP.v
        p.eop("vector", "tensor_copy", PSTG[buf], PS[:, pbk * 1024:pbk * 1024 + 1024],
              waits=[wv(SP, sp_f2[hc])] +
                    ([wv(SD, sd_part[hc - 2])] if hc >= 2 else []),
              incs=[(SV, 1)])
        sv_part[hc] = SV.v
        dma("sync", partsp[hc * 128:(hc + 1) * 128, :], PSTG[buf], SD,
            waits=[w(SV)])
        sd_part[hc] = SD.v
    HSQ = BIG[:, 32768:65536]
    UPS = [WAR[:, 0:8192], WAR[:, 8192:16384]]
    DNS = [WAR[:, 0:4096], WAR[:, 4096:8192]]
    RELUS = XSTB[:, 0:2048]
    sp_up = {}
    sp_dn = {}
    st_dn = {}
    sg_pf_up = None
    dma("gpsimd", UPS[0].rearrange("q (c f) -> q c f", c=NCH),
        upw[:, 0:256].rearrange("(c q) f -> q c f", q=128),
        SG, waits=[wv(SP, sp_f2[31])])
    sg_pf_up = SG.v
    p.barrier()
    for half in range(2):
        sv_hsq = {}
        for ip in range(16):
            buf = ip % 2
            if ip == 0 and sg_pf_up is not None:
                sg_s = sg_pf_up
                sg_pf_up = None
            else:
                waits = [wv(SP, sp_up[ip - 2])] if ip >= 2 else []
                dma("gpsimd", UPS[buf].rearrange("q (c f) -> q c f", c=NCH),
                    upw[:, half * 4096 + ip * 256:half * 4096 + (ip + 1) * 256]
                    .rearrange("(c q) f -> q c f", q=128), SG, waits=waits)
                sg_s = SG.v
            for ic2 in range(2):
                for fc in range(NCH):
                    for qb in range(2):
                        first = ic2 == 0 and fc == 0 and qb == 0
                        mm(PS[:, ic2 * 1024 + qb * 512:ic2 * 1024 + qb * 512 + 512],
                           UPS[buf][:, fc * 256 + ic2 * 128:fc * 256 + ic2 * 128 + 128],
                           ln2c(fc)[:, qb * 512:qb * 512 + 512],
                           fc == 0, fc == NCH - 1,
                           waits=(([wv(SG, sg_s)] if first else []) +
                                  ([wv(SV, sv_hsq[ip - 1])] if (ip >= 1 and first) else [])),
                           incs=[(SP, 1)] if (fc == NCH - 1 and qb == 1) else ())
            sp_up[ip] = SP.v
            for ic2 in range(2):
                p.eop("scalar", "activation", RELUS[:, ic2 * 1024:(ic2 + 1) * 1024],
                      PS[:, ic2 * 1024:ic2 * 1024 + 1024], AF.Relu,
                      waits=(([wv(SP, sp_up[ip])] if ic2 == 0 else []) +
                             ([wv(SV, sv_hsq[ip - 1])] if (ip >= 1 and ic2 == 0) else [])),
                      incs=[(SA, 1)])
                a_rl = SA.v
                p.eop("vector", "tensor_tensor",
                      HSQ[:, (ip * 2 + ic2) * 1024:(ip * 2 + ic2 + 1) * 1024],
                      RELUS[:, ic2 * 1024:(ic2 + 1) * 1024],
                      RELUS[:, ic2 * 1024:(ic2 + 1) * 1024], ALU.mult,
                      waits=[wv(SA, a_rl)],
                      incs=[(SV, 1)] if ic2 == 1 else ())
            sv_hsq[ip] = SV.v
        dma("gpsimd", DNS[0].rearrange("q (c f) -> q c f", c=NCH),
            dnw[half * 4096:half * 4096 + 4096, 0:128]
            .rearrange("(c q) f -> q c f", q=128), SG,
            waits=[wv(SP, sp_up[14])])
        sg_pf_dn = SG.v
        p.barrier()
        PIN = [XST[:, 0:1024], XST[:, 1024:2048]]
        POUT = [XST[:, 2048:3072], XST[:, 3072:4096]]
        PREB = [WAR[:, 8192:9216], WAR[:, 9216:10240]]
        PSQB = WAR[:, 10240:11264]
        sd_pin, sv_dn, sd_dn = {}, {}, {}
        for hc in range(NCH):
            buf = hc % 2
            if hc == 0:
                sg_s = sg_pf_dn
            else:
                waits = [wv(SP, sp_dn[hc - 2])] if (hc >= 2 and (hc - 2) in sp_dn) else []
                dma("gpsimd", DNS[buf].rearrange("q (c f) -> q c f", c=NCH),
                    dnw[half * 4096:half * 4096 + 4096, hc * 128:(hc + 1) * 128]
                    .rearrange("(c q) f -> q c f", q=128), SG, waits=waits)
                sg_s = SG.v
            dma("sync", PIN[buf], partsp[hc * 128:(hc + 1) * 128, :], SD,
                waits=[wv(SV, sv_dn[hc - 2])] if hc >= 2 else [])
            sd_pin[hc] = SD.v
            pbk = hc % 2
            for ic in range(NCH):
                for qb in range(2):
                    first = ic == 0 and qb == 0
                    mm(PS[:, pbk * 1024 + qb * 512:pbk * 1024 + qb * 512 + 512],
                       DNS[buf][:, ic * 128:(ic + 1) * 128],
                       HSQ[:, ic * 1024 + qb * 512:ic * 1024 + qb * 512 + 512],
                       ic == 0, ic == NCH - 1,
                       waits=(([wv(SG, sg_s)] if first else []) +
                              ([wv(SV, sv_dn[hc - 2])] if (hc >= 2 and first) else [])),
                       incs=[(SP, 1)] if (ic == NCH - 1 and qb == 1) else ())
            sp_dn[hc] = SP.v
            if half == 0:
                p.eop("vector", "tensor_tensor", POUT[buf],
                      PS[:, pbk * 1024:pbk * 1024 + 1024], PIN[buf], ALU.add,
                      waits=[wv(SP, sp_dn[hc]), wv(SD, sd_pin[hc])] +
                            ([wv(SD, sd_dn[hc - 2])] if hc >= 2 else []),
                      incs=[(SV, 1)])
                sv_dn[hc] = SV.v
                dma("sync", partsp[hc * 128:(hc + 1) * 128, :], POUT[buf], SD,
                    waits=[w(SV)])
                sd_dn[hc] = SD.v
            else:
                p.eop("vector", "tensor_tensor", PREB[buf],
                      PS[:, pbk * 1024:pbk * 1024 + 1024], PIN[buf], ALU.add,
                      waits=[wv(SP, sp_dn[hc]), wv(SD, sd_pin[hc])] +
                            ([wv(SD, sd_dn[hc - 2])] if hc >= 2 else []) +
                            ([wv(SP, st_dn[hc - 2])] if hc >= 2 else []))
                p.eop("vector", "tensor_tensor", PSQB, PREB[buf], PREB[buf],
                      ALU.mult,
                      waits=([wv(SP, st_dn[hc - 1])] if hc >= 1 else []),
                      incs=[(SV, 1)])
                sv_dn[hc] = SV.v
                dma("sync", presp[hc * 128:(hc + 1) * 128, :], PREB[buf], SD,
                    waits=[w(SV)])
                sd_dn[hc] = SD.v
                for qb in range(2):
                    mm(PS[0:1, 2048 + qb * 512:2048 + qb * 512 + 512], onesc,
                       PREB[buf][:, qb * 512:qb * 512 + 512],
                       hc == 0, hc == NCH - 1,
                       waits=[wv(SV, sv_dn[hc])] if qb == 0 else ())
                for qb in range(2):
                    mm(PS[0:1, 3072 + qb * 512:3072 + qb * 512 + 512], onesc,
                       PSQB[:, qb * 512:qb * 512 + 512],
                       hc == 0, hc == NCH - 1,
                       incs=[(SP, 1)] if qb == 1 else ())
                st_dn[hc] = SP.v
        if half == 0:
            dma("gpsimd", UPS[0].rearrange("q (c f) -> q c f", c=NCH),
                upw[:, 4096:4096 + 256].rearrange("(c q) f -> q c f", q=128),
                SG, waits=[wv(SP, sp_dn[30]), wv(SP, sp_dn[31])])
            sg_pf_up = SG.v
        p.barrier()

    R0 = WARF[0:1, 0:1024]
    R1 = WARF[0:1, 1024:2048]
    MH3 = WAR[0:1, 4096:5120]
    RH3 = WAR[0:1, 5120:6144]
    RL3 = WAR[0:1, 6144:7168]
    sp_ln3 = SP.v
    p.eop("vector", "tensor_scalar_mul", R0, PS[0:1, 2048:3072], 1.0 / DM,
          waits=[wv(SP, sp_ln3)])
    p.eop("vector", "tensor_tensor", R1, R0, R0, ALU.mult)
    p.eop("vector", "scalar_tensor_tensor", R1, PS[0:1, 3072:4096], 1.0 / DM,
          R1, ALU.mult, ALU.subtract, incs=[(SV, 1)])
    v_var3 = SV.v
    p.eop("vector", "tensor_scalar_add", R1, R1, EPS, incs=[(SV, 1)])
    v_vare3 = SV.v
    p.eop("scalar", "activation", R1, R1, AF.Ln, waits=[wv(SV, v_vare3)],
          incs=[(SA, 1)])
    p.eop("vector", "tensor_scalar_mul", R1, R1, -0.5, waits=[w(SA)],
          incs=[(SV, 1)])
    p.eop("scalar", "activation", R1, R1, AF.Exp, waits=[w(SV)], incs=[(SA, 1)])
    a_rstd3 = SA.v
    p.eop("vector", "tensor_copy", MH3, R0, waits=[wv(SA, a_rstd3)])
    p.eop("vector", "tensor_copy", RH3, R1)
    p.eop("vector", "tensor_copy", R0, RH3)
    p.eop("vector", "tensor_tensor", RL3, R1, R0, ALU.subtract, incs=[(SV, 1)])
    p.barrier()
    for qb in range(2):
        mm(PS[:, qb * 512:qb * 512 + 512], onesr,
           MH3[:, qb * 512:qb * 512 + 512], True, True)
    for qb in range(2):
        mm(PS[:, 1024 + qb * 512:1024 + qb * 512 + 512], onesr,
           RH3[:, qb * 512:qb * 512 + 512], True, False)
        mm(PS[:, 1024 + qb * 512:1024 + qb * 512 + 512], onesr,
           RL3[:, qb * 512:qb * 512 + 512], False, True,
           incs=[(SP, 1)] if qb == 1 else ())
    pe_bc3 = SP.v

    PREC = [WAR[:, 8192:9216], WAR[:, 9216:10240]]
    X2CC = [XST[:, 0:1024], XST[:, 1024:2048]]
    TMPO = XST[:, 2048:3072]
    OUTS = [XST[:, 3072:4096], WARF[:, 0:1024]]
    sd_o, sv_o = {}, {}
    for hc in range(NCH):
        buf = hc % 2
        waits = [wv(SV, sv_o[hc - 2])] if hc >= 2 else []
        dma("sync", PREC[buf], presp[hc * 128:(hc + 1) * 128, :], SD,
            waits=waits)
        dma("sync", X2CC[buf], x2sp[hc * 128:(hc + 1) * 128, :], SD)
        sd_i = SD.v
        p.eop("vector", "tensor_tensor", TMPO, PREC[buf], PS[:, 0:1024],
              ALU.subtract,
              waits=[wv(SD, sd_i)] + ([wv(SP, pe_bc3)] if hc == 0 else []))
        p.eop("vector", "tensor_tensor", TMPO, TMPO, PS[:, 1024:2048], ALU.mult)
        p.eop("vector", "tensor_scalar", TMPO, TMPO,
              biasT[:, 3 * NCH + hc:3 * NCH + hc + 1],
              biasT[:, 4 * NCH + hc:4 * NCH + hc + 1], ALU.mult, ALU.add)
        p.eop("vector", "tensor_tensor", OUTS[buf], TMPO, X2CC[buf], ALU.add,
              waits=[wv(SD, sd_o[hc - 2])] if hc >= 2 else [],
              incs=[(SV, 1)])
        sv_o[hc] = SV.v
        dma("sync", out[hc * 128:(hc + 1) * 128, :], OUTS[buf], SD,
            waits=[w(SV)])
        sd_o[hc] = SD.v
    p.op("sync", None, waits=[wv(SD, SD.v)])
    p.barrier()

    with nc.Block() as block:
        p.playback(block)
    stack.close()
    return nc


def kernel(L0_attn_x, L0_attn_Wq, L0_attn_bq, L0_attn_Wk, L0_attn_bk,
           L0_attn_Wv, L0_attn_bv, L0_attn_bias_weight, L0_ffn_gate_w,
           L0_ffn_up_w, L0_ffn_down_w, L0_ffn_fc1_w, L0_ffn_fc2_w,
           L0_ffn_expert_w, L0_ffn_ln_gamma, L0_ffn_ln_beta):
    c1, dconst, repm, maskm, pairs = _host_consts()
    nc = build_nc()

    def chunkv(v):
        return np.ascontiguousarray(np.asarray(v, np.float32).reshape(NCH, 128).T)

    base = {
        "wqT": np.ascontiguousarray(L0_attn_Wq.T),
        "wkT": np.ascontiguousarray(L0_attn_Wk.T),
        "wvT": np.ascontiguousarray(L0_attn_Wv.T),
        "bqc": chunkv(L0_attn_bq), "bkc": chunkv(L0_attn_bk),
        "bvc": chunkv(L0_attn_bv),
        "gamc": chunkv(L0_ffn_ln_gamma), "betc": chunkv(L0_ffn_ln_beta),
        "bwT": np.ascontiguousarray(L0_attn_bias_weight.T),
        "gw": np.asarray(L0_ffn_gate_w, np.float32),
        "upw": np.asarray(L0_ffn_up_w, np.float32),
        "dnw": np.asarray(L0_ffn_down_w, np.float32),
        "f1w": np.asarray(L0_ffn_fc1_w, np.float32),
        "f2w": np.asarray(L0_ffn_fc2_w, np.float32),
        "ew": np.asarray(L0_ffn_expert_w, np.float32),
        "c1": c1, "dconst": dconst, "repm": repm,
        "maskm": maskm, "ppm": pairs,
        "idn_in": np.eye(128, dtype=np.float32),
    }
    in_maps = []
    for c in range(8):
        b, qh = c // 2, c % 2
        xt = np.asarray(L0_attn_x[b], np.float32).T
        if qh == 0:
            xfm = np.ascontiguousarray(xt)
        else:
            xfm = np.ascontiguousarray(
                np.concatenate([xt[:, 1024:], xt[:, :1024]], axis=1))
        m = dict(base)
        m["xfm"] = xfm
        in_maps.append(m)
    res = run_bass_kernel_spmd(nc, in_maps, core_ids=list(range(8)))
    outp = np.empty((4, 2048, 4096), dtype=np.float32)
    for c in range(8):
        b, qh = c // 2, c % 2
        outp[b, qh * 1024:(qh + 1) * 1024, :] = res.results[c]["out"].T
    return outp
